# version 01
# baseline (speedup 1.0000x reference)
import numpy as np
import ml_dtypes
from contextlib import ExitStack

import concourse.bass as bass
import concourse.mybir as mybir
from concourse.bass_utils import run_bass_kernel_spmd

F32 = mybir.dt.float32
BF16 = mybir.dt.bfloat16
AF = mybir.ActivationFunctionType
ALU = mybir.AluOpType
AX = mybir.AxisListType
BF16NP = ml_dtypes.bfloat16

T = 16384
D = 4096
H = 4
QT = 256
WAVE = 4
NC = 8
K_PH = T // (QT * NC)
IC = D // 128
HC = (D // H) // 128
DG = D // 512
TCH = QT // 128
SSUB = QT // 128
TOK_C = K_PH * QT
NWV = T // (WAVE * QT)
EPS = 1e-5

def _p_k(k):
    return NC * (k + 1)

TOTU = sum(_p_k(k) for k in range(K_PH))

def _g_list(c):
    return [NC * k + (c if k % 2 == 0 else NC - 1 - c) for k in range(K_PH)]


class _SS:
    def __init__(self, eng, sem, v, key):
        self._eng = eng
        self._sem = sem
        self._v = v
        self._key = key

    def __getattr__(self, opname):
        eng, sem, v, key = self._eng, self._sem, self._v, self._key

        def call(*a, **kw):
            if eng is not None and v[key] > 0:
                eng.wait_ge(sem, v[key])
            inst = getattr(eng, opname)(*a, **kw) if eng is not None else None
            v[key] += 1
            if inst is not None:
                inst.then_inc(sem, 1)
            return inst
        return call


def build_A():
    nc = bass.Bass("TRN2", target_bir_lowering=False, debug=False)
    x = nc.dram_tensor("x", [TOK_C, D], F32, kind="ExternalInput").ap()
    xn1 = nc.dram_tensor("xn1", [TOK_C, D], BF16, kind="ExternalOutput").ap()
    NCH = TOK_C // 128
    with ExitStack() as ctx:
        xb = [ctx.enter_context(nc.sbuf_tensor(f"xb{i}", [128, D], F32)) for i in range(2)]
        ob = [ctx.enter_context(nc.sbuf_tensor(f"ob{i}", [128, D], BF16)) for i in range(2)]
        bn = ctx.enter_context(nc.sbuf_tensor("bn", [128, D // 512, 6], F32))
        st2 = ctx.enter_context(nc.sbuf_tensor("st2", [128, 2], F32))
        sd = ctx.enter_context(nc.sbuf_tensor("sd", [128, 1], F32))
        rr = ctx.enter_context(nc.sbuf_tensor("rr", [128, 1], F32))
        eps = ctx.enter_context(nc.sbuf_tensor("eps", [128, 1], F32))
        s_l = [ctx.enter_context(nc.semaphore(f"s_l{i}")) for i in range(2)]
        s_v = ctx.enter_context(nc.semaphore("s_v"))
        s_s = ctx.enter_context(nc.semaphore("s_s"))
        s_o = [ctx.enter_context(nc.semaphore(f"s_o{i}")) for i in range(2)]
        blk = ctx.enter_context(nc.Block())

        NSG = D // 512
        PER = NSG + 3
        def vpos(t, j):
            return 1 + PER * t + j + 1

        @blk.sync
        def _(sync):
            for t in range(NCH):
                sync.wait_ge(s_l[t % 2], 16 * (t // 2))
                if t >= 2:
                    sync.wait_ge(s_v, vpos(t - 2, PER - 1))
                sync.dma_start(out=xb[t % 2][:], in_=x[t * 128:(t + 1) * 128, :]).then_inc(s_l[t % 2], 16)

        @blk.vector
        def _(vector):
            cnt = 0
            def op(inst):
                nonlocal cnt
                cnt += 1
                inst.then_inc(s_v, 1)
            vector.memset(eps[:], EPS).then_inc(s_v, 1); cnt = 1
            for t in range(NCH):
                vector.wait_ge(s_l[t % 2], 16 * (t // 2 + 1))
                if t >= 2:
                    vector.wait_ge(s_o[t % 2], 16 * (t // 2))
                for j in range(D // 512):
                    vector.wait_ge(s_v, cnt)
                    op(vector.bn_stats(bn[:, j, :], xb[t % 2][:, j * 512:(j + 1) * 512]))
                vector.wait_ge(s_v, cnt)
                op(vector.bn_aggr(st2[:], bn[:].rearrange("p a b -> p (a b)")))
                vector.wait_ge(s_s, t + 1)
                vector.wait_ge(s_v, cnt)
                op(vector.reciprocal(rr[:], sd[:]))
                vector.wait_ge(s_v, cnt)
                op(vector.tensor_scalar(ob[t % 2][:], xb[t % 2][:], st2[:, 0:1], rr[:],
                                        ALU.subtract, ALU.mult))

        @blk.scalar
        def _(scalar):
            for t in range(NCH):
                scalar.wait_ge(s_v, vpos(t, NSG))
                if t >= 1:
                    scalar.wait_ge(s_s, t)
                scalar.activation(sd[:], st2[:, 1:2], AF.Sqrt, bias=eps[:]).then_inc(s_s, 1)

        @blk.gpsimd
        def _(gpsimd):
            for t in range(NCH):
                gpsimd.wait_ge(s_o[t % 2], 16 * (t // 2))
                gpsimd.wait_ge(s_v, vpos(t, PER - 1))
                gpsimd.dma_start(out=xn1[t * 128:(t + 1) * 128, :], in_=ob[t % 2][:]).then_inc(s_o[t % 2], 16)
            gpsimd.wait_ge(s_o[0], 16 * ((NCH + 1) // 2))
            gpsimd.wait_ge(s_o[1], 16 * (NCH // 2))
    return nc


def build_B():
    nc = bass.Bass("TRN2", target_bir_lowering=False, debug=False)
    ap = {}
    ap["xn1T"] = nc.dram_tensor("xn1T", [T // QT, IC, 128, QT], BF16, kind="ExternalInput").ap()
    ap["xn1w"] = nc.dram_tensor("xn1w", [NWV, DG, WAVE * QT, 512], BF16, kind="ExternalInput").ap()
    ap["xq"] = nc.dram_tensor("xq", [K_PH, QT, D], F32, kind="ExternalInput").ap()
    ap["xqT"] = nc.dram_tensor("xqT", [K_PH, IC, 128, QT], F32, kind="ExternalInput").ap()
    ap["masks"] = nc.dram_tensor("masks", [TOTU, QT, QT], BF16, kind="ExternalInput").ap()
    ap["bvq"] = nc.dram_tensor("bvq", [H, D // H], F32, kind="ExternalInput").ap()
    ap["bvk"] = nc.dram_tensor("bvk", [H, D // H], F32, kind="ExternalInput").ap()
    ap["bvv"] = nc.dram_tensor("bvv", [H, D // H], F32, kind="ExternalInput").ap()
    ap["iden"] = nc.dram_tensor("iden", [128, 128], F32, kind="ExternalInput").ap()
    ap["x1o"] = nc.dram_tensor("x1o", [K_PH, QT, D], F32, kind="ExternalOutput").ap()
    ap["xn2o"] = nc.dram_tensor("xn2o", [K_PH, QT, D], BF16, kind="ExternalOutput").ap()
    ap["parto"] = nc.dram_tensor("parto", [128, IC], F32, kind="ExternalOutput").ap()
    ap["qkb_d"] = nc.dram_tensor("qkb_d", [D], F32, kind="Internal").ap()
    ap["vb_d"] = nc.dram_tensor("vb_d", [D], F32, kind="Internal").ap()

    with ExitStack() as ctx:
        sb = {}
        def S(name, shape, dt):
            sb[name] = ctx.enter_context(nc.sbuf_tensor(name, shape, dt))
        S("kt0", [128, IC, QT], BF16); S("kt1", [128, IC, QT], BF16)
        S("qt", [128, IC, QT], BF16)
        S("aw", [128, WAVE, H, SSUB, QT], BF16)
        S("acc", [128, TCH, D], F32)
        S("vw0", [128, WAVE * QT // 128, 512], BF16)
        S("vw1", [128, WAVE * QT // 128, 512], BF16)
        S("xqn", [128, D], F32)
        S("lns", [128, D], F32)
        S("xn2s", [128, D], BF16)
        S("xqTc0", [128, QT], F32); S("xqTc1", [128, QT], F32)
        S("tmpc", [128, QT], F32); S("rqc", [128, QT], F32)
        S("mk0", [128, SSUB, QT], BF16); S("mk1", [128, SSUB, QT], BF16)
        S("vbbc", [128, D], BF16)
        S("mb", [128, QT], F32); S("rb", [128, QT], F32)
        S("qkb", [128, IC], F32)
        S("bvq_s", [H, D // H], F32); S("bvk_s", [H, D // H], F32); S("bvv_s", [H, D // H], F32)
        S("al", [H, 1], F32); S("al2", [H, 1], F32)
        S("geb", [H, D // H], F32)
        S("qb", [H, D // H], F32); S("kb", [H, D // H], F32)
        S("bn", [128, D // 512, 6], F32)
        S("st2", [128, 2], F32); S("sd", [128, 1], F32); S("rr", [128, 1], F32)
        S("mrow", [1, QT], F32); S("rrow", [1, QT], F32)
        S("vrow", [1, D], BF16)
        S("ones_r", [1, 128], F32)
        S("ones_rb", [1, 128], BF16)
        S("ones_cb", [128, 1], BF16)
        S("eps", [128, 1], F32)
        S("iden_s", [128, 128], F32)
        S("parts", [128, IC], F32)

        ps = {}
        def P(name, shape, dt):
            ps[name] = ctx.enter_context(nc.psum_tensor(name, shape, dt))
        for i in range(4):
            P(f"sc{i}", [128, 512], F32)
        P("av0", [128, 512], F32); P("av1", [128, 512], F32)
        P("stat", [1, 512], F32)
        P("part", [128, IC], F32)

        sem_names = ["ld", "gd", "kt_l0", "kt_l1", "mk_l0", "mk_l1",
                     "vw_l0", "vw_l1", "od",
                     "sc_d", "av_d",
                     "mv", "msc", "mp"]
        sems = {n: ctx.enter_context(nc.semaphore(n)) for n in sem_names}
        blk = ctx.enter_context(nc.Block())

        def walker(E):
            v = {n: 0 for n in sem_names}

            def on(name):
                return E.get(name) is not None

            def inc(sem, inst, n=1):
                v[sem] += n
                if inst is not None:
                    inst.then_inc(sems[sem], n)

            def w(name, sem, val):
                if on(name) and val > 0:
                    E[name].wait_ge(sems[sem], val)

            def dma(sem, engname, fn):
                w(engname, sem, v[sem])
                inst = fn() if on(engname) else None
                inc(sem, inst, 16)
                return v[sem]

            sy, ve, se, te, ge = (E.get(n) for n in ("sync", "vector", "scalar", "tensor", "gpsimd"))
            vv = _SS(ve, sems["mv"], v, "mv")
            ss = _SS(se, sems["msc"], v, "msc")

            for name, dst in (("bvq", "bvq_s"), ("bvk", "bvk_s"), ("bvv", "bvv_s"), ("iden", "iden_s")):
                dma("ld", "sync", lambda name=name, dst=dst: sy.dma_start(out=sb[dst][:], in_=ap[name]))

            vv.memset(sb["ones_r"][:], 1.0)
            vv.memset(sb["ones_rb"][:], 1.0)
            vv.memset(sb["ones_cb"][:], 1.0)
            vv.memset(sb["eps"][:], EPS)
            w("vector", "ld", 64)
            for (srcn, dstn) in (("bvq_s", "qb"), ("bvk_s", "kb")):
                vv.reduce_sum(sb["al"][:], sb[srcn][:], AX.X, apply_absolute_value=True)
                vv.tensor_scalar(sb["al"][:], sb["al"][:], float(H) / D, None, ALU.mult)
                vv.tensor_scalar(sb["al2"][:], sb["al"][:], 2.0, None, ALU.mult)
                vv.tensor_single_scalar(sb["geb"][:], sb[srcn][:], 0.0, ALU.is_ge)
                vv.tensor_scalar(sb[dstn][:], sb["geb"][:], sb["al2"][:], sb["al"][:],
                                 ALU.mult, ALU.subtract)
            vv.tensor_mul(sb["qb"][:], sb["qb"][:], sb["kb"][:])
            vv.reduce_sum(sb["al"][:], sb["bvv_s"][:], AX.X, apply_absolute_value=True)
            vv.tensor_scalar(sb["al"][:], sb["al"][:], float(H) / D, None, ALU.mult)
            vv.tensor_scalar(sb["al2"][:], sb["al"][:], 2.0, None, ALU.mult)
            vv.tensor_single_scalar(sb["geb"][:], sb["bvv_s"][:], 0.0, ALU.is_ge)
            vv.tensor_scalar(sb["kb"][:], sb["geb"][:], sb["al2"][:], sb["al"][:],
                             ALU.mult, ALU.subtract)
            w("gpsimd", "mv", v["mv"])
            dma("gd", "gpsimd", lambda: ge.dma_start(
                out=ap["qkb_d"].rearrange("(h j) -> h j", h=H), in_=sb["qb"][:]))
            dma("gd", "gpsimd", lambda: ge.dma_start(
                out=ap["vb_d"].rearrange("(h j) -> h j", h=H), in_=sb["kb"][:]))
            for ic in range(IC):
                dma("gd", "gpsimd", lambda ic=ic: ge.dma_start(
                    out=sb["qkb"][:, ic:ic + 1],
                    in_=ap["qkb_d"][ic * 128:(ic + 1) * 128].rearrange("(p o) -> p o", o=1)))
            gd_vrow = dma("gd", "gpsimd", lambda: ge.dma_start(
                out=sb["vrow"][:], in_=ap["vb_d"].rearrange("(o d) -> o d", o=1)))
            w("tensor", "gd", gd_vrow)
            for j in range(D // 512):
                inc("mp", te.matmul(ps["av0"][:], sb["ones_rb"][:],
                                    sb["vrow"][0:1, j * 512:(j + 1) * 512],
                                    start=True, stop=True) if on("tensor") else None)
                w("scalar", "mp", v["mp"])
                ss.copy(sb["vbbc"][:, j * 512:(j + 1) * 512], ps["av0"][:])
                w("tensor", "msc", v["msc"])

            U = 0
            GW = 0
            part_first = True
            adds_mv = 0
            od_epi = 0
            od_xn2_prev = 0
            sig_done = {}
            mask_done = {}
            add_done = {}
            for k in range(K_PH):
                Pk = _p_k(k)
                w("tensor", "msc", v["msc"])
                for tch in range(TCH):
                    w("gpsimd", "mv", v["mv"])
                    gd_xqn = dma("gd", "gpsimd", lambda k=k, tch=tch: ge.dma_start(
                        out=sb["xqn"][:], in_=ap["xq"][k, tch * 128:(tch + 1) * 128, :]))
                    w("vector", "gd", gd_xqn)
                    for j in range(D // 512):
                        vv.bn_stats(sb["bn"][:, j, :], sb["xqn"][:, j * 512:(j + 1) * 512])
                    vv.bn_aggr(sb["st2"][:], sb["bn"][:].rearrange("p a b -> p (a b)"))
                    w("scalar", "mv", v["mv"])
                    ss.activation(sb["sd"][:], sb["st2"][:, 1:2], AF.Sqrt, bias=sb["eps"][:])
                    w("vector", "msc", v["msc"])
                    vv.reciprocal(sb["rr"][:], sb["sd"][:])
                    w("tensor", "mv", v["mv"])
                    inc("mp", te.transpose(ps["stat"][0:1, tch * 128:tch * 128 + 128],
                                           sb["st2"][:, 0:1], sb["iden_s"][:])
                        if on("tensor") else None)
                    inc("mp", te.transpose(ps["stat"][0:1, 256 + tch * 128:256 + tch * 128 + 128],
                                           sb["rr"][:], sb["iden_s"][:])
                        if on("tensor") else None)
                w("scalar", "mp", v["mp"])
                ss.copy(sb["mrow"][:], ps["stat"][0:1, 0:QT])
                ss.copy(sb["rrow"][:], ps["stat"][0:1, 256:256 + QT])
                w("tensor", "msc", v["msc"])
                w("tensor", "mv", adds_mv)
                inc("mp", te.matmul(ps["av0"][:, 0:QT], sb["ones_r"][:], sb["mrow"][:],
                                    start=True, stop=True) if on("tensor") else None)
                inc("mp", te.matmul(ps["av1"][:, 0:QT], sb["ones_r"][:], sb["rrow"][:],
                                    start=True, stop=True) if on("tensor") else None)
                w("scalar", "mp", v["mp"])
                ss.copy(sb["mb"][:], ps["av0"][:, 0:QT])
                ss.copy(sb["rb"][:], ps["av1"][:, 0:QT])
                msc_mbrb = v["msc"]
                mv_qb = v["mv"]
                sc_prev = v["sc_d"]
                for ic in range(IC):
                    if ic >= 2:
                        w("gpsimd", "mv", mv_qb + 3 * (ic - 1))
                    gd_ch = dma("gd", "gpsimd", lambda k=k, ic=ic: ge.dma_start(
                        out=sb["xqTc%d" % (ic % 2)][:], in_=ap["xqT"][k, ic]))
                    w("vector", "gd", gd_ch)
                    if ic == 0:
                        w("vector", "msc", msc_mbrb)
                        w("vector", "sc_d", sc_prev)
                    vv.tensor_sub(sb["tmpc"][:], sb["xqTc%d" % (ic % 2)][:], sb["mb"][:])
                    vv.tensor_mul(sb["rqc"][:], sb["qkb"][:, ic:ic + 1].to_broadcast([128, QT]),
                                  sb["rb"][:])
                    vv.tensor_mul(sb["qt"][:, ic, :], sb["tmpc"][:], sb["rqc"][:])
                qt_ready = v["mv"]

                NG = H * SSUB
                for u in range(Pk):
                    uw = u % WAVE
                    ktp = "kt_l%d" % (U % 2)
                    mkp = "mk_l%d" % (U % 2)
                    w("sync", "sc_d", NG * (U - 1) if U >= 2 else 0)
                    kt_done = dma(ktp, "sync", lambda u=u, U=U: sy.dma_start(
                        out=sb["kt%d" % (U % 2)][:],
                        in_=ap["xn1T"][u].rearrange("c p t -> p c t")))
                    if U >= 2:
                        w("sync", "mv", mask_done[U - 2])
                    seq = sum(_p_k(kk) for kk in range(k)) + u
                    mk_done = dma(mkp, "sync", lambda seq=seq, U=U: sy.dma_start(
                        out=sb["mk%d" % (U % 2)][:],
                        in_=ap["masks"][seq].rearrange("(ss p) t -> p ss t", p=128)))
                    w("tensor", ktp, kt_done)
                    if u == 0:
                        w("tensor", "mv", qt_ready)
                    for g in range(NG):
                        h, ssub = divmod(g, SSUB)
                        b = g % 4
                        G = U * NG + g
                        if G >= 4:
                            w("tensor", "msc", sig_done[G - 4])
                        last = None
                        for j in range(HC):
                            ic = h * HC + j
                            i = te.matmul(ps["sc%d" % b][:, 0:QT],
                                          sb["kt%d" % (U % 2)][:, ic, ssub * 128:ssub * 128 + 128],
                                          sb["qt"][:, ic, :],
                                          start=(j == 0), stop=(j == HC - 1)) if on("tensor") else None
                            last = i
                        inc("sc_d", last)
                        if g == 0:
                            w("scalar", "av_d", DG * TCH * GW)
                        w("scalar", "sc_d", v["sc_d"])
                        ss.activation(sb["aw"][:, uw, h, ssub, :], ps["sc%d" % b][:, 0:QT],
                                      AF.Sigmoid, scale=0.125)
                        sig_done[G] = v["msc"]
                        w("vector", "msc", v["msc"])
                        w("vector", mkp, mk_done)
                        vv.tensor_mul(sb["aw"][:, uw, h, ssub, :],
                                      sb["aw"][:, uw, h, ssub, :],
                                      sb["mk%d" % (U % 2)][:, ssub, :])
                    mask_done[U] = v["mv"]
                    if uw == WAVE - 1:
                        wv_first = (u == WAVE - 1)
                        J0 = u - WAVE + 1
                        lw = J0 // WAVE
                        for dg in range(DG):
                            X = GW * DG + dg
                            vwp = "vw_l%d" % (X % 2)
                            w("scalar", "av_d", TCH * (X - 1) if X >= 2 else 0)
                            vw_done = dma(vwp, "scalar", lambda lw=lw, dg=dg, X=X: se.dma_start(
                                out=sb["vw%d" % (X % 2)][:],
                                in_=ap["xn1w"][lw, dg].rearrange("(sc p) c -> p sc c", p=128)))
                            hh = dg // (DG // H)
                            for tch in range(TCH):
                                A = X * TCH + tch
                                b2 = A % 2
                                w("tensor", vwp, vw_done)
                                if tch == 0 and dg == 0:
                                    w("tensor", "mv", mask_done[U])
                                    w("tensor", "msc", msc_mbrb)
                                if A >= 2:
                                    w("tensor", "mv", add_done[A - 2])
                                last = None
                                for sc in range(WAVE * SSUB):
                                    uu, ssub = divmod(sc, SSUB)
                                    i = te.matmul(ps["av%d" % b2][:],
                                                  sb["aw"][:, uu, hh, ssub, tch * 128:tch * 128 + 128],
                                                  sb["vw%d" % (X % 2)][:, sc, :],
                                                  start=(sc == 0), stop=(sc == WAVE * SSUB - 1)) \
                                        if on("tensor") else None
                                    last = i
                                inc("av_d", last)
                                w("vector", "av_d", v["av_d"])
                                if wv_first:
                                    w("vector", "od", od_epi)
                                    vv.tensor_copy(sb["acc"][:, tch, dg * 512:(dg + 1) * 512],
                                                   ps["av%d" % b2][:])
                                else:
                                    vv.tensor_add(sb["acc"][:, tch, dg * 512:(dg + 1) * 512],
                                                  ps["av%d" % b2][:],
                                                  sb["acc"][:, tch, dg * 512:(dg + 1) * 512])
                                add_done[A] = v["mv"]
                        GW += 1
                    U += 1
                adds_mv = v["mv"]

                for tch in range(TCH):
                    vv.tensor_mul(sb["acc"][:, tch, :], sb["acc"][:, tch, :], sb["vbbc"][:])
                    for j in range(D // 512):
                        vv.bn_stats(sb["bn"][:, j, :], sb["acc"][:, tch, j * 512:(j + 1) * 512])
                    vv.bn_aggr(sb["st2"][:], sb["bn"][:].rearrange("p a b -> p (a b)"))
                    w("scalar", "mv", v["mv"])
                    ss.activation(sb["sd"][:], sb["st2"][:, 1:2], AF.Sqrt, bias=sb["eps"][:])
                    w("vector", "msc", v["msc"])
                    vv.reciprocal(sb["rr"][:], sb["sd"][:])
                    vv.tensor_scalar(sb["lns"][:], sb["acc"][:, tch, :], sb["st2"][:, 0:1],
                                     sb["rr"][:], ALU.subtract, ALU.mult)
                    w("gpsimd", "mv", v["mv"])
                    gd_xqe = dma("gd", "gpsimd", lambda k=k, tch=tch: ge.dma_start(
                        out=sb["xqn"][:], in_=ap["xq"][k, tch * 128:(tch + 1) * 128, :]))
                    w("vector", "gd", gd_xqe)
                    vv.tensor_add(sb["acc"][:, tch, :], sb["lns"][:], sb["xqn"][:])
                    w("gpsimd", "mv", v["mv"])
                    dma("od", "gpsimd", lambda k=k, tch=tch: ge.dma_start(
                        out=ap["x1o"][k, tch * 128:(tch + 1) * 128, :], in_=sb["acc"][:, tch, :]))
                    for j in range(D // 512):
                        vv.bn_stats(sb["bn"][:, j, :], sb["acc"][:, tch, j * 512:(j + 1) * 512])
                    vv.bn_aggr(sb["st2"][:], sb["bn"][:].rearrange("p a b -> p (a b)"))
                    w("scalar", "mv", v["mv"])
                    ss.activation(sb["sd"][:], sb["st2"][:, 1:2], AF.Sqrt, bias=sb["eps"][:])
                    w("vector", "msc", v["msc"])
                    vv.reciprocal(sb["rr"][:], sb["sd"][:])
                    w("vector", "mp", v["mp"])
                    w("vector", "od", od_xn2_prev)
                    vv.tensor_scalar(sb["xn2s"][:], sb["acc"][:, tch, :], sb["st2"][:, 0:1],
                                     sb["rr"][:], ALU.subtract, ALU.mult)
                    w("gpsimd", "mv", v["mv"])
                    od_xn2_prev = dma("od", "gpsimd", lambda k=k, tch=tch: ge.dma_start(
                        out=ap["xn2o"][k, tch * 128:(tch + 1) * 128, :], in_=sb["xn2s"][:]))
                    w("tensor", "mv", v["mv"])
                    last = None
                    for ic in range(IC):
                        i = te.matmul(ps["part"][:, ic:ic + 1],
                                      sb["xn2s"][:, ic * 128:(ic + 1) * 128],
                                      sb["ones_cb"][:],
                                      start=part_first,
                                      stop=(k == K_PH - 1 and tch == TCH - 1 and ic == IC - 1),
                                      skip_group_check=True) if on("tensor") else None
                        last = i
                        part_first = False
                    inc("mp", last)
                od_epi = v["od"]

            w("scalar", "mp", v["mp"])
            ss.copy(sb["parts"][:], ps["part"][:])
            w("gpsimd", "msc", v["msc"])
            dma("od", "gpsimd", lambda: ge.dma_start(out=ap["parto"], in_=sb["parts"][:]))
            w("gpsimd", "od", v["od"])

        @blk.sync
        def _(sync):
            walker({"sync": sync})

        @blk.vector
        def _(vector):
            walker({"vector": vector})

        @blk.scalar
        def _(scalar):
            walker({"scalar": scalar})

        @blk.tensor
        def _(tensor):
            walker({"tensor": tensor})

        @blk.gpsimd
        def _(gpsimd):
            walker({"gpsimd": gpsimd})

    return nc


def build_C():
    NT = TOK_C // 128
    NEXP = 8
    nc = bass.Bass("TRN2", target_bir_lowering=False, debug=False)
    ap = {}
    ap["xn2T"] = nc.dram_tensor("xn2T", [NT, IC, 128, 128], BF16, kind="ExternalInput").ap()
    ap["x1"] = nc.dram_tensor("x1", [TOK_C, D], F32, kind="ExternalInput").ap()
    ap["parts"] = nc.dram_tensor("parts", [128, NC * IC], F32, kind="ExternalInput").ap()
    ap["wt"] = nc.dram_tensor("wt", [DG, IC, 128, 512], F32, kind="ExternalInput").ap()
    ap["down"] = nc.dram_tensor("down", [IC, 128, 512], F32, kind="ExternalInput").ap()
    ap["up"] = nc.dram_tensor("up", [4, 128, D], F32, kind="ExternalInput").ap()
    ap["wqT"] = nc.dram_tensor("wqT", [IC, 128, 128], F32, kind="ExternalInput").ap()
    ap["keysT"] = nc.dram_tensor("keysT", [128, NEXP], F32, kind="ExternalInput").ap()
    ap["idenb"] = nc.dram_tensor("idenb", [128, 128], BF16, kind="ExternalInput").ap()
    ap["out"] = nc.dram_tensor("out", [TOK_C, D], F32, kind="ExternalOutput").ap()
    ap["fscr"] = nc.dram_tensor("fscr", [TOK_C, D], F32, kind="Internal").ap()

    with ExitStack() as ctx:
        sb = {}
        def S(name, shape, dt):
            sb[name] = ctx.enter_context(nc.sbuf_tensor(name, shape, dt))
        S("xtt0", [128, IC, 128], BF16); S("xtt1", [128, IC, 128], BF16)
        S("dwn", [128, IC, 512], BF16)
        S("wbf", [128, IC, 512], BF16)
        S("upsc", [128, 4, D], BF16)
        S("upstb", [128, D], BF16)
        S("h", [128, NT, 512], BF16)
        S("hT", [128, 4, NT, 128], BF16)
        S("fst0", [128, 512], F32); S("fst1", [128, 512], F32)
        S("sqs", [128, 512], F32)
        S("red", [128, NT, DG, 2], F32)
        S("prow", [128, NC * IC], F32)
        S("meanc", [128, IC], F32)
        S("q", [128, 1], F32)
        S("scr", [1, NEXP], F32)
        S("cf", [128, NEXP], F32)
        S("keys", [128, NEXP], F32)
        S("fln", [128, D], F32)
        S("x1c", [128, D], F32)
        S("st_s", [128, 1], F32); S("st_q", [128, 1], F32)
        S("mn", [128, 1], F32); S("vr", [128, 1], F32)
        S("sd", [128, 1], F32); S("rr", [128, 1], F32)
        S("eps", [128, 1], F32)
        S("mx", [1, 1], F32); S("sm", [1, 1], F32); S("rs", [1, 1], F32)
        S("iden_b", [128, 128], BF16)
        S("ones_r", [1, 128], F32)
        ps = {}
        def P(name, shape, dt):
            ps[name] = ctx.enter_context(nc.psum_tensor(name, shape, dt))
        P("fb0", [128, 512], F32); P("fb1", [128, 512], F32)
        P("hp", [128, 512], F32)
        P("tp", [128, 128], BF16)
        P("qp", [128, 1], F32)
        P("sp", [1, NEXP], F32)
        P("bc", [128, NEXP], F32)

        sem_names = ["ld", "gd", "mp", "mv", "msc", "od", "xt_l0", "xt_l1", "w_l", "fs"]
        sems = {n: ctx.enter_context(nc.semaphore(n)) for n in sem_names}
        blk = ctx.enter_context(nc.Block())

        def walker(E):
            v = {n: 0 for n in sem_names}

            def on(name):
                return E.get(name) is not None

            def inc(sem, inst, n=1):
                v[sem] += n
                if inst is not None:
                    inst.then_inc(sems[sem], n)

            def w(name, sem, val):
                if on(name) and val > 0:
                    E[name].wait_ge(sems[sem], val)

            def dma(sem, engname, fn):
                w(engname, sem, v[sem])
                inst = fn() if on(engname) else None
                inc(sem, inst, 16)
                return v[sem]

            sy, ve, se, te, ge = (E.get(n) for n in ("sync", "vector", "scalar", "tensor", "gpsimd"))
            vv = _SS(ve, sems["mv"], v, "mv")
            ss = _SS(se, sems["msc"], v, "msc")

            dma("ld", "sync", lambda: sy.dma_start(out=sb["prow"][:], in_=ap["parts"]))
            dma("ld", "sync", lambda: sy.dma_start(out=sb["keys"][:], in_=ap["keysT"]))
            dma("ld", "sync", lambda: sy.dma_start(out=sb["iden_b"][:], in_=ap["idenb"]))
            ld_setup = dma("ld", "sync", lambda: sy.dma_start(
                out=sb["fln"][:].rearrange("p (c t) -> p c t", t=128),
                in_=ap["wqT"].rearrange("c p t -> p c t")))
            gd_dwn = dma("gd", "gpsimd", lambda: ge.dma_start(
                out=sb["dwn"][:], in_=ap["down"].rearrange("c p t -> p c t")))
            vv.memset(sb["eps"][:], EPS)
            vv.memset(sb["ones_r"][:], 1.0)
            w("vector", "ld", 16)
            sz = NC * IC
            while sz > IC:
                half = sz // 2
                vv.tensor_add(sb["prow"][:, 0:half], sb["prow"][:, 0:half],
                              sb["prow"][:, half:2 * half])
                sz = half
            vv.tensor_copy(sb["meanc"][:], sb["prow"][:, 0:IC])
            w("tensor", "ld", ld_setup)
            w("tensor", "mv", v["mv"])
            last = None
            for ic in range(IC):
                i = te.matmul(ps["qp"][:], sb["fln"][:, ic * 128:(ic + 1) * 128],
                              sb["meanc"][:, ic:ic + 1],
                              start=(ic == 0), stop=(ic == IC - 1)) if on("tensor") else None
                last = i
            inc("mp", last)
            w("scalar", "mp", v["mp"])
            scl = 1.0 / (float(T) * float(np.sqrt(128.0)))
            ss.activation(sb["q"][:], ps["qp"][:], AF.Copy, scale=scl)
            w("tensor", "msc", v["msc"])
            inc("mp", te.matmul(ps["sp"][:], sb["q"][:], sb["keys"][:], start=True, stop=True)
                if on("tensor") else None)
            w("scalar", "mp", v["mp"])
            ss.copy(sb["scr"][:], ps["sp"][:])
            w("vector", "msc", v["msc"])
            vv.reduce_max(sb["mx"][:], sb["scr"][:], AX.X, negate=True)
            w("scalar", "mv", v["mv"])
            ss.activation(sb["scr"][:], sb["scr"][:], AF.Exp, bias=sb["mx"][:])
            w("vector", "msc", v["msc"])
            vv.reduce_sum(sb["sm"][:], sb["scr"][:], AX.X)
            vv.reciprocal(sb["rs"][:], sb["sm"][:])
            vv.tensor_scalar(sb["scr"][:], sb["scr"][:], sb["rs"][:], None, ALU.mult)
            w("tensor", "mv", v["mv"])
            inc("mp", te.matmul(ps["bc"][:], sb["ones_r"][:], sb["scr"][:], start=True, stop=True)
                if on("tensor") else None)
            w("scalar", "mp", v["mp"])
            ss.copy(sb["cf"][:], ps["bc"][:])
            msc_cf = v["msc"]
            for ch in range(4):
                if ch >= 1:
                    w("gpsimd", "mv", v["mv"])
                gd_up = dma("gd", "gpsimd", lambda ch=ch: ge.dma_start(
                    out=sb["upstb"][:], in_=ap["up"][ch]))
                w("vector", "gd", gd_up)
                if ch == 0:
                    w("vector", "msc", msc_cf)
                for hf in range(2):
                    ee = ch * 2 + hf
                    vv.tensor_scalar(sb["upsc"][hf * 64:(hf + 1) * 64, ch, :],
                                     sb["upstb"][hf * 64:(hf + 1) * 64, :],
                                     sb["cf"][hf * 64:(hf + 1) * 64, ee:ee + 1], None,
                                     ALU.mult)

            upsc_ready = v["mv"]
            w("tensor", "gd", gd_dwn)
            mp_at_tile = []
            for tt in range(NT):
                xtp = "xt_l%d" % (tt % 2)
                if tt >= 2:
                    w("sync", "mp", mp_at_tile[tt - 2])
                xt_done = dma(xtp, "sync", lambda tt=tt: sy.dma_start(
                    out=sb["xtt%d" % (tt % 2)][:],
                    in_=ap["xn2T"][tt].rearrange("c p t -> p c t")))
                w("tensor", xtp, xt_done)
                last = None
                for ic in range(IC):
                    i = te.matmul(ps["hp"][:], sb["xtt%d" % (tt % 2)][:, ic, :], sb["dwn"][:, ic, :],
                                  start=(ic == 0), stop=(ic == IC - 1)) if on("tensor") else None
                    last = i
                inc("mp", last)
                mp_at_tile.append(v["mp"])
                w("scalar", "mp", v["mp"])
                ss.copy(sb["h"][:, tt, :], ps["hp"][:])
                w("tensor", "msc", v["msc"])
            for tt in range(NT):
                for ec in range(4):
                    inc("mp", te.transpose(ps["tp"][:], sb["h"][:, tt, ec * 128:(ec + 1) * 128],
                                           sb["iden_b"][:]) if on("tensor") else None)
                    w("scalar", "mp", v["mp"])
                    ss.copy(sb["hT"][:, ec, tt, :], ps["tp"][:])
                    w("tensor", "msc", v["msc"])

            mp_og_prev = 0
            mp_fb = {}
            msc_fb = {}
            fs_done = {}
            gt = 0
            for og in range(DG):
                w("gpsimd", "mp", mp_og_prev)
                w_done = dma("w_l", "gpsimd", lambda og=og: ge.dma_start(
                    out=sb["wbf"][:], in_=ap["wt"][og].rearrange("c p t -> p c t")))
                for tt in range(NT):
                    xtp = "xt_l%d" % (tt % 2)
                    if gt >= 2:
                        w("sync", "mp", mp_fb[gt - 2])
                    elif og == 0:
                        w("sync", "mp", mp_at_tile[NT - 2 + tt])
                    xt_done = dma(xtp, "sync", lambda tt=tt: sy.dma_start(
                        out=sb["xtt%d" % (tt % 2)][:],
                        in_=ap["xn2T"][tt].rearrange("c p t -> p c t")))
                    bb = tt % 2
                    w("tensor", xtp, xt_done)
                    w("tensor", "w_l", w_done)
                    if gt == 0:
                        w("tensor", "mv", upsc_ready)
                    if gt >= 2:
                        w("tensor", "msc", msc_fb[gt - 2])
                    last = None
                    for ic in range(IC):
                        i = te.matmul(ps["fb%d" % bb][:], sb["xtt%d" % (tt % 2)][:, ic, :],
                                      sb["wbf"][:, ic, :], start=(ic == 0), stop=False) \
                            if on("tensor") else None
                    for ec in range(4):
                        i = te.matmul(ps["fb%d" % bb][:], sb["hT"][:, ec, tt, :],
                                      sb["upsc"][:, ec, og * 512:(og + 1) * 512],
                                      start=False, stop=(ec == 3)) if on("tensor") else None
                        last = i
                    inc("mp", last)
                    mp_fb[gt] = v["mp"]
                    w("scalar", "mp", v["mp"])
                    if gt >= 2:
                        w("scalar", "fs", fs_done[gt - 2])
                    ss.copy(sb["fst%d" % bb][:], ps["fb%d" % bb][:])
                    if gt >= 1:
                        w("scalar", "mv", mv_sqs_prev)
                    ss.activation(sb["sqs"][:], sb["fst%d" % bb][:], AF.Square)
                    msc_fb[gt] = v["msc"]
                    w("vector", "msc", v["msc"])
                    vv.reduce_sum(sb["red"][:, tt, og, 0:1], sb["fst%d" % bb][:], AX.X)
                    vv.reduce_sum(sb["red"][:, tt, og, 1:2], sb["sqs"][:], AX.X)
                    mv_sqs_prev = v["mv"]
                    w("gpsimd", "mv", v["mv"])
                    fs_done[gt] = dma("fs", "gpsimd", lambda tt=tt, og=og, bb=bb: ge.dma_start(
                        out=ap["fscr"][tt * 128:(tt + 1) * 128, og * 512:(og + 1) * 512],
                        in_=sb["fst%d" % bb][:]))
                    gt += 1
                mp_og_prev = v["mp"]

            mv_p2 = 0
            for tt in range(NT):
                if tt == 0:
                    w("sync", "fs", v["fs"])
                if tt >= 1:
                    w("sync", "mv", mv_p2)
                    w("sync", "od", v["od"])
                dma("ld", "sync", lambda tt=tt: sy.dma_start(
                    out=sb["fln"][:], in_=ap["fscr"][tt * 128:(tt + 1) * 128, :]))
                ld_p2 = dma("ld", "sync", lambda tt=tt: sy.dma_start(
                    out=sb["x1c"][:], in_=ap["x1"][tt * 128:(tt + 1) * 128, :]))
                w("vector", "ld", ld_p2)
                vv.reduce_sum(sb["st_s"][:], sb["red"][:, tt, :, 0], AX.X)
                vv.reduce_sum(sb["st_q"][:], sb["red"][:, tt, :, 1], AX.X)
                vv.tensor_scalar(sb["mn"][:], sb["st_s"][:], 1.0 / D, None, ALU.mult)
                vv.tensor_scalar(sb["vr"][:], sb["st_q"][:], 1.0 / D, None, ALU.mult)
                vv.tensor_mul(sb["st_s"][:], sb["mn"][:], sb["mn"][:])
                vv.tensor_sub(sb["vr"][:], sb["vr"][:], sb["st_s"][:])
                w("scalar", "mv", v["mv"])
                ss.activation(sb["sd"][:], sb["vr"][:], AF.Sqrt, bias=sb["eps"][:])
                w("vector", "msc", v["msc"])
                vv.reciprocal(sb["rr"][:], sb["sd"][:])
                vv.tensor_scalar(sb["fln"][:], sb["fln"][:], sb["mn"][:], sb["rr"][:],
                                 ALU.subtract, ALU.mult)
                vv.tensor_add(sb["x1c"][:], sb["fln"][:], sb["x1c"][:])
                mv_p2 = v["mv"]
                w("gpsimd", "mv", v["mv"])
                dma("od", "gpsimd", lambda tt=tt: ge.dma_start(
                    out=ap["out"][tt * 128:(tt + 1) * 128, :], in_=sb["x1c"][:]))
            w("gpsimd", "od", v["od"])

        @blk.sync
        def _(sync):
            walker({"sync": sync})

        @blk.vector
        def _(vector):
            walker({"vector": vector})

        @blk.scalar
        def _(scalar):
            walker({"scalar": scalar})

        @blk.tensor
        def _(tensor):
            walker({"tensor": tensor})

        @blk.gpsimd
        def _(gpsimd):
            walker({"gpsimd": gpsimd})

    return nc


def kernel(**inputs):
    x = np.ascontiguousarray(np.asarray(inputs["L0_attn_x"], dtype=np.float32).reshape(T, D))
    bvq = np.ascontiguousarray(np.asarray(inputs["L0_attn_bv_q"], dtype=np.float32))
    bvk = np.ascontiguousarray(np.asarray(inputs["L0_attn_bv_k"], dtype=np.float32))
    bvv = np.ascontiguousarray(np.asarray(inputs["L0_attn_bv_v"], dtype=np.float32))
    W = np.asarray(inputs["L0_ffn_W_org"], dtype=np.float32)
    ldn = np.asarray(inputs["L0_ffn_lora_down"], dtype=np.float32)
    lup = np.asarray(inputs["L0_ffn_lora_up"], dtype=np.float32)
    lk = np.asarray(inputs["L0_ffn_lora_keys"], dtype=np.float32)
    wq = np.asarray(inputs["L0_ffn_Wq"], dtype=np.float32)

    ncA = build_A()
    inA = [{"x": np.ascontiguousarray(x[c * TOK_C:(c + 1) * TOK_C])} for c in range(NC)]
    resA = run_bass_kernel_spmd(ncA, inA, core_ids=list(range(NC)))
    xn1 = np.concatenate([resA.results[c]["xn1"] for c in range(NC)], axis=0)

    xn1T = np.ascontiguousarray(xn1.T)
    xn1T_t = np.ascontiguousarray(
        xn1T.reshape(IC, 128, T // QT, QT).transpose(2, 0, 1, 3))
    xn1w = np.ascontiguousarray(
        xn1.reshape(NWV, WAVE * QT, DG, 512).transpose(0, 2, 1, 3))
    iden = np.eye(128, dtype=np.float32)
    inB = []
    tril = np.triu(np.ones((QT, QT), dtype=np.float32)).astype(BF16NP)
    for c in range(NC):
        gl = _g_list(c)
        xq = np.ascontiguousarray(
            np.stack([x[g * QT:(g + 1) * QT] for g in gl], axis=0))
        xqT = np.ascontiguousarray(
            np.stack([x[g * QT:(g + 1) * QT].T.reshape(IC, 128, QT) for g in gl], axis=0))
        masks = np.zeros((TOTU, QT, QT), dtype=BF16NP)
        seq = 0
        for k in range(K_PH):
            g = gl[k]
            masks[seq:seq + g] = BF16NP(1.0)
            masks[seq + g] = tril
            seq += _p_k(k)
        inB.append({
            "xn1T": xn1T_t, "xn1w": xn1w, "xq": xq, "xqT": xqT, "masks": masks,
            "bvq": bvq, "bvk": bvk, "bvv": bvv, "iden": iden,
        })
    ncB = build_B()
    resB = run_bass_kernel_spmd(ncB, inB, core_ids=list(range(NC)))

    parts = np.ascontiguousarray(np.concatenate(
        [resB.results[c]["parto"] for c in range(NC)], axis=1))
    WTt = np.ascontiguousarray(W.reshape(DG, 512, IC, 128).transpose(0, 2, 3, 1))
    down_t = np.ascontiguousarray(ldn.transpose(1, 0, 2).reshape(IC, 128, NC * 64))
    up_r = np.ascontiguousarray(lup.reshape(NC * 64, D).reshape(4, 128, D))
    wqT = np.ascontiguousarray(wq.T.reshape(IC, 128, 128))
    keysT = np.ascontiguousarray(lk.T)
    idenb = np.eye(128, dtype=BF16NP)
    inC = []
    for c in range(NC):
        x1 = np.ascontiguousarray(resB.results[c]["x1o"].reshape(TOK_C, D))
        xn2 = resB.results[c]["xn2o"].reshape(TOK_C, D)
        xn2T_t = np.ascontiguousarray(
            xn2.reshape(TOK_C // 128, 128, IC, 128).transpose(0, 2, 3, 1))
        inC.append({
            "xn2T": xn2T_t, "x1": x1, "parts": parts, "wt": WTt, "down": down_t,
            "up": up_r, "wqT": wqT, "keysT": keysT, "idenb": idenb,
        })
    ncC = build_C()
    resC = run_bass_kernel_spmd(ncC, inC, core_ids=list(range(NC)))

    out = np.empty((T, D), dtype=np.float32)
    for c in range(NC):
        gl = _g_list(c)
        oc = resC.results[c]["out"].reshape(K_PH, QT, D)
        for k in range(K_PH):
            out[gl[k] * QT:(gl[k] + 1) * QT] = oc[k]
    return out.reshape(1, T, D)


# version 02
# speedup vs baseline: 1.2544x; 1.2544x over previous
import numpy as np
import ml_dtypes
from contextlib import ExitStack

import concourse.bass as bass
import concourse.mybir as mybir
from concourse.bass_utils import run_bass_kernel_spmd

F32 = mybir.dt.float32
BF16 = mybir.dt.bfloat16
AF = mybir.ActivationFunctionType
ALU = mybir.AluOpType
AX = mybir.AxisListType
BF16NP = ml_dtypes.bfloat16

T = 16384
D = 4096
H = 4
QT = 256
WAVE = 4
NC = 8
K_PH = T // (QT * NC)
IC = D // 128
HC = (D // H) // 128
DG = D // 512
TCH = QT // 128
SSUB = QT // 128
TOK_C = K_PH * QT
NWV = T // (WAVE * QT)
EPS = 1e-5

def _p_k(k):
    return NC * (k + 1)

TOTU = sum(_p_k(k) for k in range(K_PH))

def _g_list(c):
    return [NC * k + (c if k % 2 == 0 else NC - 1 - c) for k in range(K_PH)]


class _SS:
    def __init__(self, eng, sem, v, key):
        self._eng = eng
        self._sem = sem
        self._v = v
        self._key = key

    def __getattr__(self, opname):
        eng, sem, v, key = self._eng, self._sem, self._v, self._key

        def call(*a, **kw):
            if eng is not None and v[key] > 0:
                eng.wait_ge(sem, v[key])
            inst = getattr(eng, opname)(*a, **kw) if eng is not None else None
            v[key] += 1
            if inst is not None:
                inst.then_inc(sem, 1)
            return inst
        return call


def build_A():
    nc = bass.Bass("TRN2", target_bir_lowering=False, debug=False)
    x = nc.dram_tensor("x", [TOK_C, D], F32, kind="ExternalInput").ap()
    xn1 = nc.dram_tensor("xn1", [TOK_C, D], BF16, kind="ExternalOutput").ap()
    NCH = TOK_C // 128
    with ExitStack() as ctx:
        xb = [ctx.enter_context(nc.sbuf_tensor(f"xb{i}", [128, D], F32)) for i in range(2)]
        ob = [ctx.enter_context(nc.sbuf_tensor(f"ob{i}", [128, D], BF16)) for i in range(2)]
        bn = ctx.enter_context(nc.sbuf_tensor("bn", [128, D // 512, 6], F32))
        st2 = ctx.enter_context(nc.sbuf_tensor("st2", [128, 2], F32))
        sd = ctx.enter_context(nc.sbuf_tensor("sd", [128, 1], F32))
        rr = ctx.enter_context(nc.sbuf_tensor("rr", [128, 1], F32))
        eps = ctx.enter_context(nc.sbuf_tensor("eps", [128, 1], F32))
        s_l = [ctx.enter_context(nc.semaphore(f"s_l{i}")) for i in range(2)]
        s_v = ctx.enter_context(nc.semaphore("s_v"))
        s_s = ctx.enter_context(nc.semaphore("s_s"))
        s_o = [ctx.enter_context(nc.semaphore(f"s_o{i}")) for i in range(2)]
        blk = ctx.enter_context(nc.Block())

        NSG = D // 512
        PER = NSG + 3
        def vpos(t, j):
            return 1 + PER * t + j + 1

        @blk.sync
        def _(sync):
            for t in range(NCH):
                sync.wait_ge(s_l[t % 2], 16 * (t // 2))
                if t >= 2:
                    sync.wait_ge(s_v, vpos(t - 2, PER - 1))
                sync.dma_start(out=xb[t % 2][:], in_=x[t * 128:(t + 1) * 128, :]).then_inc(s_l[t % 2], 16)

        @blk.vector
        def _(vector):
            cnt = 0
            def op(inst):
                nonlocal cnt
                cnt += 1
                inst.then_inc(s_v, 1)
            vector.memset(eps[:], EPS).then_inc(s_v, 1); cnt = 1
            for t in range(NCH):
                vector.wait_ge(s_l[t % 2], 16 * (t // 2 + 1))
                if t >= 2:
                    vector.wait_ge(s_o[t % 2], 16 * (t // 2))
                for j in range(D // 512):
                    vector.wait_ge(s_v, cnt)
                    op(vector.bn_stats(bn[:, j, :], xb[t % 2][:, j * 512:(j + 1) * 512]))
                vector.wait_ge(s_v, cnt)
                op(vector.bn_aggr(st2[:], bn[:].rearrange("p a b -> p (a b)")))
                vector.wait_ge(s_s, t + 1)
                vector.wait_ge(s_v, cnt)
                op(vector.reciprocal(rr[:], sd[:]))
                vector.wait_ge(s_v, cnt)
                op(vector.tensor_scalar(ob[t % 2][:], xb[t % 2][:], st2[:, 0:1], rr[:],
                                        ALU.subtract, ALU.mult))

        @blk.scalar
        def _(scalar):
            for t in range(NCH):
                scalar.wait_ge(s_v, vpos(t, NSG))
                if t >= 1:
                    scalar.wait_ge(s_s, t)
                scalar.activation(sd[:], st2[:, 1:2], AF.Sqrt, bias=eps[:]).then_inc(s_s, 1)

        @blk.gpsimd
        def _(gpsimd):
            for t in range(NCH):
                gpsimd.wait_ge(s_o[t % 2], 16 * (t // 2))
                gpsimd.wait_ge(s_v, vpos(t, PER - 1))
                gpsimd.dma_start(out=xn1[t * 128:(t + 1) * 128, :], in_=ob[t % 2][:]).then_inc(s_o[t % 2], 16)
            gpsimd.wait_ge(s_o[0], 16 * ((NCH + 1) // 2))
            gpsimd.wait_ge(s_o[1], 16 * (NCH // 2))
    return nc


def build_B():
    nc = bass.Bass("TRN2", target_bir_lowering=False, debug=False)
    ap = {}
    ap["xn1T"] = nc.dram_tensor("xn1T", [T // QT, IC, 128, QT], BF16, kind="ExternalInput").ap()
    ap["xn1w"] = nc.dram_tensor("xn1w", [NWV, DG, WAVE * QT, 512], BF16, kind="ExternalInput").ap()
    ap["xq"] = nc.dram_tensor("xq", [K_PH, QT, D], F32, kind="ExternalInput").ap()
    ap["xqT"] = nc.dram_tensor("xqT", [K_PH, IC, 128, QT], F32, kind="ExternalInput").ap()
    ap["masks"] = nc.dram_tensor("masks", [TOTU, QT, QT], BF16, kind="ExternalInput").ap()
    ap["bvq"] = nc.dram_tensor("bvq", [H, D // H], F32, kind="ExternalInput").ap()
    ap["bvk"] = nc.dram_tensor("bvk", [H, D // H], F32, kind="ExternalInput").ap()
    ap["bvv"] = nc.dram_tensor("bvv", [H, D // H], F32, kind="ExternalInput").ap()
    ap["iden"] = nc.dram_tensor("iden", [128, 128], F32, kind="ExternalInput").ap()
    ap["x1o"] = nc.dram_tensor("x1o", [K_PH, QT, D], F32, kind="ExternalOutput").ap()
    ap["xn2o"] = nc.dram_tensor("xn2o", [K_PH, QT, D], BF16, kind="ExternalOutput").ap()
    ap["parto"] = nc.dram_tensor("parto", [128, IC], F32, kind="ExternalOutput").ap()
    ap["qkb_d"] = nc.dram_tensor("qkb_d", [D], F32, kind="Internal").ap()
    ap["vb_d"] = nc.dram_tensor("vb_d", [D], F32, kind="Internal").ap()

    with ExitStack() as ctx:
        sb = {}
        def S(name, shape, dt):
            sb[name] = ctx.enter_context(nc.sbuf_tensor(name, shape, dt))
        S("kt0", [128, IC, QT], BF16); S("kt1", [128, IC, QT], BF16)
        S("qt", [128, IC, QT], BF16)
        S("aw", [128, WAVE, H, SSUB, QT], BF16)
        S("acc", [128, TCH, D], F32)
        S("vw0", [128, WAVE * QT // 128, 512], BF16)
        S("vw1", [128, WAVE * QT // 128, 512], BF16)
        S("xqn", [128, D], F32)
        S("xn2s", [128, D], BF16)
        S("rqkb", [128, IC, QT], BF16)
        S("mk0", [128, SSUB, QT], BF16); S("mk1", [128, SSUB, QT], BF16)
        S("vbbc", [128, D], BF16)
        S("mb", [128, QT], F32); S("rb", [128, QT], F32)
        S("qkb", [128, IC], F32)
        S("bvq_s", [H, D // H], F32); S("bvk_s", [H, D // H], F32); S("bvv_s", [H, D // H], F32)
        S("al", [H, 1], F32); S("al2", [H, 1], F32)
        S("geb", [H, D // H], F32)
        S("qb", [H, D // H], F32); S("kb", [H, D // H], F32)
        S("bn", [128, D // 512, 6], F32)
        S("st2", [128, 2], F32); S("sd", [128, 1], F32); S("rr", [128, 1], F32)
        S("mrow", [1, QT], F32); S("rrow", [1, QT], F32)
        S("vrow", [1, D], BF16)
        S("ones_r", [1, 128], F32)
        S("ones_rb", [1, 128], BF16)
        S("ones_cb", [128, 1], BF16)
        S("eps", [128, 1], F32)
        S("iden_s", [128, 128], F32)
        S("parts", [128, IC], F32)

        ps = {}
        def P(name, shape, dt):
            ps[name] = ctx.enter_context(nc.psum_tensor(name, shape, dt))
        for i in range(4):
            P(f"sc{i}", [128, 512], F32)
        P("av0", [128, 512], F32); P("av1", [128, 512], F32)
        P("stat", [1, 512], F32)
        P("part", [128, IC], F32)

        sem_names = ["ld", "gd", "kt_l0", "kt_l1", "mk_l0", "mk_l1",
                     "vw_l0", "vw_l1", "od",
                     "sc_d", "av_d",
                     "mv", "msc", "mp"]
        sems = {n: ctx.enter_context(nc.semaphore(n)) for n in sem_names}
        blk = ctx.enter_context(nc.Block())

        def walker(E):
            v = {n: 0 for n in sem_names}

            def on(name):
                return E.get(name) is not None

            def inc(sem, inst, n=1):
                v[sem] += n
                if inst is not None:
                    inst.then_inc(sems[sem], n)

            def w(name, sem, val):
                if on(name) and val > 0:
                    E[name].wait_ge(sems[sem], val)

            def dma(sem, engname, fn):
                w(engname, sem, v[sem])
                inst = fn() if on(engname) else None
                inc(sem, inst, 16)
                return v[sem]

            sy, ve, se, te, ge = (E.get(n) for n in ("sync", "vector", "scalar", "tensor", "gpsimd"))
            vv = _SS(ve, sems["mv"], v, "mv")
            ss = _SS(se, sems["msc"], v, "msc")

            for name, dst in (("bvq", "bvq_s"), ("bvk", "bvk_s"), ("bvv", "bvv_s"), ("iden", "iden_s")):
                dma("ld", "sync", lambda name=name, dst=dst: sy.dma_start(out=sb[dst][:], in_=ap[name]))

            vv.memset(sb["ones_r"][:], 1.0)
            vv.memset(sb["ones_rb"][:], 1.0)
            vv.memset(sb["ones_cb"][:], 1.0)
            vv.memset(sb["eps"][:], EPS)
            w("vector", "ld", 64)
            for (srcn, dstn) in (("bvq_s", "qb"), ("bvk_s", "kb")):
                vv.reduce_sum(sb["al"][:], sb[srcn][:], AX.X, apply_absolute_value=True)
                vv.tensor_scalar(sb["al"][:], sb["al"][:], float(H) / D, None, ALU.mult)
                vv.tensor_scalar(sb["al2"][:], sb["al"][:], 2.0, None, ALU.mult)
                vv.tensor_single_scalar(sb["geb"][:], sb[srcn][:], 0.0, ALU.is_ge)
                vv.tensor_scalar(sb[dstn][:], sb["geb"][:], sb["al2"][:], sb["al"][:],
                                 ALU.mult, ALU.subtract)
            vv.tensor_mul(sb["qb"][:], sb["qb"][:], sb["kb"][:])
            vv.reduce_sum(sb["al"][:], sb["bvv_s"][:], AX.X, apply_absolute_value=True)
            vv.tensor_scalar(sb["al"][:], sb["al"][:], float(H) / D, None, ALU.mult)
            vv.tensor_scalar(sb["al2"][:], sb["al"][:], 2.0, None, ALU.mult)
            vv.tensor_single_scalar(sb["geb"][:], sb["bvv_s"][:], 0.0, ALU.is_ge)
            vv.tensor_scalar(sb["kb"][:], sb["geb"][:], sb["al2"][:], sb["al"][:],
                             ALU.mult, ALU.subtract)
            w("gpsimd", "mv", v["mv"])
            dma("gd", "gpsimd", lambda: ge.dma_start(
                out=ap["qkb_d"].rearrange("(h j) -> h j", h=H), in_=sb["qb"][:]))
            dma("gd", "gpsimd", lambda: ge.dma_start(
                out=ap["vb_d"].rearrange("(h j) -> h j", h=H), in_=sb["kb"][:]))
            for ic in range(IC):
                dma("gd", "gpsimd", lambda ic=ic: ge.dma_start(
                    out=sb["qkb"][:, ic:ic + 1],
                    in_=ap["qkb_d"][ic * 128:(ic + 1) * 128].rearrange("(p o) -> p o", o=1)))
            gd_vrow = dma("gd", "gpsimd", lambda: ge.dma_start(
                out=sb["vrow"][:], in_=ap["vb_d"].rearrange("(o d) -> o d", o=1)))
            w("tensor", "gd", gd_vrow)
            for j in range(D // 512):
                inc("mp", te.matmul(ps["av0"][:], sb["ones_rb"][:],
                                    sb["vrow"][0:1, j * 512:(j + 1) * 512],
                                    start=True, stop=True) if on("tensor") else None)
                w("scalar", "mp", v["mp"])
                ss.copy(sb["vbbc"][:, j * 512:(j + 1) * 512], ps["av0"][:])
                w("tensor", "msc", v["msc"])

            st = {"U": 0, "GW": 0, "part_first": True,
                  "od_epi": 0, "od_xn2_prev": 0}
            sig_done = {}
            mask_done = {}
            add_done = {}
            vw_done = {}
            qt_ready = {}
            NG = H * SSUB
            NSLOT = min(NG, 8)

            def sc_slot(g):
                b = g % 4
                off = (g // 4) * QT
                return ps["sc%d" % b][:, off:off + QT]

            def emit_prologue(k):
                w("tensor", "msc", v["msc"])
                for tch in range(TCH):
                    w("gpsimd", "mv", v["mv"])
                    gd_xqn = dma("gd", "gpsimd", lambda k=k, tch=tch: ge.dma_start(
                        out=sb["xqn"][:], in_=ap["xq"][k, tch * 128:(tch + 1) * 128, :]))
                    w("vector", "gd", gd_xqn)
                    for j in range(D // 512):
                        vv.bn_stats(sb["bn"][:, j, :], sb["xqn"][:, j * 512:(j + 1) * 512])
                    vv.bn_aggr(sb["st2"][:], sb["bn"][:].rearrange("p a b -> p (a b)"))
                    w("scalar", "mv", v["mv"])
                    ss.activation(sb["sd"][:], sb["st2"][:, 1:2], AF.Sqrt, bias=sb["eps"][:])
                    w("vector", "msc", v["msc"])
                    vv.reciprocal(sb["rr"][:], sb["sd"][:])
                    w("tensor", "mv", v["mv"])
                    inc("mp", te.transpose(ps["stat"][0:1, tch * 128:tch * 128 + 128],
                                           sb["st2"][:, 0:1], sb["iden_s"][:])
                        if on("tensor") else None)
                    inc("mp", te.transpose(ps["stat"][0:1, 256 + tch * 128:256 + tch * 128 + 128],
                                           sb["rr"][:], sb["iden_s"][:])
                        if on("tensor") else None)
                w("scalar", "mp", v["mp"])
                ss.copy(sb["mrow"][:], ps["stat"][0:1, 0:QT])
                ss.copy(sb["rrow"][:], ps["stat"][0:1, 256:256 + QT])
                w("tensor", "msc", v["msc"])
                w("tensor", "mv", st.get("adds_mv", 0))
                inc("mp", te.matmul(ps["av0"][:, 0:QT], sb["ones_r"][:], sb["mrow"][:],
                                    start=True, stop=True) if on("tensor") else None)
                inc("mp", te.matmul(ps["av1"][:, 0:QT], sb["ones_r"][:], sb["rrow"][:],
                                    start=True, stop=True) if on("tensor") else None)
                w("scalar", "mp", v["mp"])
                ss.copy(sb["mb"][:], ps["av0"][:, 0:QT])
                ss.copy(sb["rb"][:], ps["av1"][:, 0:QT])
                msc_mbrb = v["msc"]
                w("gpsimd", "sc_d", v["sc_d"])
                gd_qt = dma("gd", "gpsimd", lambda k=k: ge.dma_start(
                    out=sb["qt"][:], in_=ap["xqT"][k].rearrange("c p t -> p c t")))
                w("vector", "gd", gd_qt)
                w("vector", "msc", msc_mbrb)
                vv.tensor_copy(sb["rqkb"][:], sb["rb"][:].unsqueeze(1).to_broadcast([128, IC, QT]))
                vv.tensor_mul(sb["rqkb"][:], sb["rqkb"][:],
                              sb["qkb"][:].unsqueeze(2).to_broadcast([128, IC, QT]))
                vv.tensor_sub(sb["qt"][:], sb["qt"][:],
                              sb["mb"][:].unsqueeze(1).to_broadcast([128, IC, QT]))
                vv.tensor_mul(sb["qt"][:], sb["qt"][:], sb["rqkb"][:])
                qt_ready[k] = v["mv"]
                st["msc_mbrb"] = msc_mbrb

            def emit_units(k):
                Pk = _p_k(k)
                msc_mbrb = st["msc_mbrb"]
                for u in range(Pk):
                    U = st["U"]
                    GW = st["GW"]
                    uw = u % WAVE
                    ktp = "kt_l%d" % (U % 2)
                    mkp = "mk_l%d" % (U % 2)
                    w("sync", "sc_d", NG * (U - 1) if U >= 2 else 0)
                    kt_done = dma(ktp, "sync", lambda u=u, U=U: sy.dma_start(
                        out=sb["kt%d" % (U % 2)][:],
                        in_=ap["xn1T"][u].rearrange("c p t -> p c t")))
                    if U >= 2:
                        w("gpsimd", "mv", mask_done[U - 2])
                    seq = sum(_p_k(kk) for kk in range(k)) + u
                    mk_done = dma(mkp, "gpsimd", lambda seq=seq, U=U: ge.dma_start(
                        out=sb["mk%d" % (U % 2)][:],
                        in_=ap["masks"][seq].rearrange("(ss p) t -> p ss t", p=128)))
                    if uw == 0:
                        for dg in range(min(2, DG)):
                            X = GW * DG + dg
                            vwp = "vw_l%d" % (X % 2)
                            w("scalar", "av_d", TCH * (X - 1) if X >= 2 else 0)
                            vw_done[X] = dma(vwp, "scalar", lambda lw=u // WAVE, dg=dg, X=X:
                                             se.dma_start(out=sb["vw%d" % (X % 2)][:],
                                                          in_=ap["xn1w"][lw, dg]
                                                          .rearrange("(sc p) c -> p sc c", p=128)))
                    w("tensor", ktp, kt_done)
                    if u == 0:
                        w("tensor", "mv", qt_ready[k])
                    if U >= 1:
                        w("tensor", "msc", sig_done[U * NG - 1])
                    for g in range(NG):
                        h, ssub = divmod(g, SSUB)
                        G = U * NG + g
                        last = None
                        for j in range(HC):
                            ic = h * HC + j
                            i = te.matmul(sc_slot(g),
                                          sb["kt%d" % (U % 2)][:, ic, ssub * 128:ssub * 128 + 128],
                                          sb["qt"][:, ic, :],
                                          start=(j == 0), stop=(j == HC - 1)) if on("tensor") else None
                            last = i
                        inc("sc_d", last)
                        if g == 0:
                            w("scalar", "av_d", DG * TCH * GW)
                            w("vector", mkp, mk_done)
                        w("scalar", "sc_d", v["sc_d"])
                        ss.activation(sb["aw"][:, uw, h, ssub, :], sc_slot(g),
                                      AF.Sigmoid, scale=0.125)
                        sig_done[G] = v["msc"]
                        w("vector", "msc", v["msc"])
                        vv.tensor_mul(sb["aw"][:, uw, h, ssub, :],
                                      sb["aw"][:, uw, h, ssub, :],
                                      sb["mk%d" % (U % 2)][:, ssub, :])
                    mask_done[U] = v["mv"]
                    if uw == WAVE - 1:
                        wv_first = (u == WAVE - 1)
                        lw = (u - WAVE + 1) // WAVE
                        for dg in range(DG):
                            X = GW * DG + dg
                            if dg + 2 < DG:
                                X2 = X + 2
                                vwp2 = "vw_l%d" % (X2 % 2)
                                w("scalar", "av_d", TCH * (X2 - 1) if X2 >= 2 else 0)
                                vw_done[X2] = dma(vwp2, "scalar", lambda lw=lw, dg=dg + 2, X2=X2:
                                                  se.dma_start(out=sb["vw%d" % (X2 % 2)][:],
                                                               in_=ap["xn1w"][lw, dg]
                                                               .rearrange("(sc p) c -> p sc c", p=128)))
                            vwp = "vw_l%d" % (X % 2)
                            hh = dg // (DG // H)
                            for tch in range(TCH):
                                A = X * TCH + tch
                                b2 = A % 2
                                if tch == 0:
                                    w("tensor", vwp, vw_done[X])
                                if tch == 0 and dg == 0:
                                    w("tensor", "mv", mask_done[U])
                                    w("tensor", "msc", msc_mbrb)
                                if A >= 2:
                                    w("tensor", "mv", add_done[A - 2])
                                last = None
                                for sc in range(WAVE * SSUB):
                                    uu, ssub = divmod(sc, SSUB)
                                    i = te.matmul(ps["av%d" % b2][:],
                                                  sb["aw"][:, uu, hh, ssub, tch * 128:tch * 128 + 128],
                                                  sb["vw%d" % (X % 2)][:, sc, :],
                                                  start=(sc == 0), stop=(sc == WAVE * SSUB - 1)) \
                                        if on("tensor") else None
                                    last = i
                                inc("av_d", last)
                                w("vector", "av_d", v["av_d"])
                                if wv_first:
                                    w("vector", "od", st["od_epi"])
                                    vv.tensor_copy(sb["acc"][:, tch, dg * 512:(dg + 1) * 512],
                                                   ps["av%d" % b2][:])
                                else:
                                    vv.tensor_add(sb["acc"][:, tch, dg * 512:(dg + 1) * 512],
                                                  ps["av%d" % b2][:],
                                                  sb["acc"][:, tch, dg * 512:(dg + 1) * 512])
                                add_done[A] = v["mv"]
                        st["GW"] = GW + 1
                    st["U"] = U + 1
                st["adds_mv"] = v["mv"]

            def emit_epilogue(k):
                for tch in range(TCH):
                    vv.tensor_mul(sb["acc"][:, tch, :], sb["acc"][:, tch, :], sb["vbbc"][:])
                    for j in range(D // 512):
                        vv.bn_stats(sb["bn"][:, j, :], sb["acc"][:, tch, j * 512:(j + 1) * 512])
                    vv.bn_aggr(sb["st2"][:], sb["bn"][:].rearrange("p a b -> p (a b)"))
                    w("scalar", "mv", v["mv"])
                    ss.activation(sb["sd"][:], sb["st2"][:, 1:2], AF.Sqrt, bias=sb["eps"][:])
                    w("vector", "msc", v["msc"])
                    vv.reciprocal(sb["rr"][:], sb["sd"][:])
                    w("gpsimd", "mv", v["mv"])
                    gd_xqe = dma("gd", "gpsimd", lambda k=k, tch=tch: ge.dma_start(
                        out=sb["xqn"][:], in_=ap["xq"][k, tch * 128:(tch + 1) * 128, :]))
                    vv.tensor_scalar(sb["acc"][:, tch, :], sb["acc"][:, tch, :], sb["st2"][:, 0:1],
                                     sb["rr"][:], ALU.subtract, ALU.mult)
                    w("vector", "gd", gd_xqe)
                    vv.tensor_add(sb["acc"][:, tch, :], sb["acc"][:, tch, :], sb["xqn"][:])
                    w("gpsimd", "mv", v["mv"])
                    dma("od", "gpsimd", lambda k=k, tch=tch: ge.dma_start(
                        out=ap["x1o"][k, tch * 128:(tch + 1) * 128, :], in_=sb["acc"][:, tch, :]))
                st["od_epi"] = v["od"]
                for tch in range(TCH):
                    for j in range(D // 512):
                        vv.bn_stats(sb["bn"][:, j, :], sb["acc"][:, tch, j * 512:(j + 1) * 512])
                    vv.bn_aggr(sb["st2"][:], sb["bn"][:].rearrange("p a b -> p (a b)"))
                    w("scalar", "mv", v["mv"])
                    ss.activation(sb["sd"][:], sb["st2"][:, 1:2], AF.Sqrt, bias=sb["eps"][:])
                    w("vector", "msc", v["msc"])
                    vv.reciprocal(sb["rr"][:], sb["sd"][:])
                    w("vector", "mp", v["mp"])
                    w("vector", "od", st["od_xn2_prev"])
                    vv.tensor_scalar(sb["xn2s"][:], sb["acc"][:, tch, :], sb["st2"][:, 0:1],
                                     sb["rr"][:], ALU.subtract, ALU.mult)
                    w("gpsimd", "mv", v["mv"])
                    st["od_xn2_prev"] = dma("od", "gpsimd", lambda k=k, tch=tch: ge.dma_start(
                        out=ap["xn2o"][k, tch * 128:(tch + 1) * 128, :], in_=sb["xn2s"][:]))
                    w("tensor", "mv", v["mv"])
                    last = None
                    for ic in range(IC):
                        i = te.matmul(ps["part"][:, ic:ic + 1],
                                      sb["xn2s"][:, ic * 128:(ic + 1) * 128],
                                      sb["ones_cb"][:],
                                      start=st["part_first"],
                                      stop=(k == K_PH - 1 and tch == TCH - 1 and ic == IC - 1),
                                      skip_group_check=True) if on("tensor") else None
                        last = i
                        st["part_first"] = False
                    inc("mp", last)

            emit_prologue(0)
            for k in range(K_PH):
                emit_units(k)
                if k + 1 < K_PH:
                    emit_prologue(k + 1)
                emit_epilogue(k)

            w("scalar", "mp", v["mp"])
            ss.copy(sb["parts"][:], ps["part"][:])
            w("gpsimd", "msc", v["msc"])
            dma("od", "gpsimd", lambda: ge.dma_start(out=ap["parto"], in_=sb["parts"][:]))
            w("gpsimd", "od", v["od"])

        @blk.sync
        def _(sync):
            walker({"sync": sync})

        @blk.vector
        def _(vector):
            walker({"vector": vector})

        @blk.scalar
        def _(scalar):
            walker({"scalar": scalar})

        @blk.tensor
        def _(tensor):
            walker({"tensor": tensor})

        @blk.gpsimd
        def _(gpsimd):
            walker({"gpsimd": gpsimd})

    return nc


def build_C():
    NT = TOK_C // 128
    NEXP = 8
    nc = bass.Bass("TRN2", target_bir_lowering=False, debug=False)
    ap = {}
    ap["xn2T"] = nc.dram_tensor("xn2T", [NT, IC, 128, 128], BF16, kind="ExternalInput").ap()
    ap["x1"] = nc.dram_tensor("x1", [TOK_C, D], F32, kind="ExternalInput").ap()
    ap["parts"] = nc.dram_tensor("parts", [128, NC * IC], F32, kind="ExternalInput").ap()
    ap["wt"] = nc.dram_tensor("wt", [DG, IC, 128, 512], F32, kind="ExternalInput").ap()
    ap["down"] = nc.dram_tensor("down", [IC, 128, 512], F32, kind="ExternalInput").ap()
    ap["up"] = nc.dram_tensor("up", [4, 128, D], F32, kind="ExternalInput").ap()
    ap["wqT"] = nc.dram_tensor("wqT", [IC, 128, 128], F32, kind="ExternalInput").ap()
    ap["keysT"] = nc.dram_tensor("keysT", [128, NEXP], F32, kind="ExternalInput").ap()
    ap["idenb"] = nc.dram_tensor("idenb", [128, 128], BF16, kind="ExternalInput").ap()
    ap["out"] = nc.dram_tensor("out", [TOK_C, D], F32, kind="ExternalOutput").ap()
    ap["fscr"] = nc.dram_tensor("fscr", [TOK_C, D], F32, kind="Internal").ap()

    with ExitStack() as ctx:
        sb = {}
        def S(name, shape, dt):
            sb[name] = ctx.enter_context(nc.sbuf_tensor(name, shape, dt))
        S("xtt0", [128, IC, 128], BF16); S("xtt1", [128, IC, 128], BF16)
        S("dwn", [128, IC, 512], BF16)
        S("wbf", [128, IC, 512], BF16)
        S("upsc", [128, 4, D], BF16)
        S("upstb", [128, D], BF16)
        S("h", [128, NT, 512], BF16)
        S("hT", [128, 4, NT, 128], BF16)
        S("fst0", [128, 512], F32); S("fst1", [128, 512], F32)
        S("sqs", [128, 512], F32)
        S("red", [128, NT, DG, 2], F32)
        S("prow", [128, NC * IC], F32)
        S("meanc", [128, IC], F32)
        S("q", [128, 1], F32)
        S("scr", [1, NEXP], F32)
        S("cf", [128, NEXP], F32)
        S("keys", [128, NEXP], F32)
        S("fln", [128, D], F32)
        S("x1c", [128, D], F32)
        S("st_s", [128, 1], F32); S("st_q", [128, 1], F32)
        S("mn", [128, 1], F32); S("vr", [128, 1], F32)
        S("sd", [128, 1], F32); S("rr", [128, 1], F32)
        S("eps", [128, 1], F32)
        S("mx", [1, 1], F32); S("sm", [1, 1], F32); S("rs", [1, 1], F32)
        S("iden_b", [128, 128], BF16)
        S("ones_r", [1, 128], F32)
        ps = {}
        def P(name, shape, dt):
            ps[name] = ctx.enter_context(nc.psum_tensor(name, shape, dt))
        P("fb0", [128, 512], F32); P("fb1", [128, 512], F32)
        P("hp", [128, 512], F32)
        P("tp", [128, 128], BF16)
        P("qp", [128, 1], F32)
        P("sp", [1, NEXP], F32)
        P("bc", [128, NEXP], F32)

        sem_names = ["ld", "gd", "mp", "mv", "msc", "od", "xt_l0", "xt_l1", "w_l", "fs"]
        sems = {n: ctx.enter_context(nc.semaphore(n)) for n in sem_names}
        blk = ctx.enter_context(nc.Block())

        def walker(E):
            v = {n: 0 for n in sem_names}

            def on(name):
                return E.get(name) is not None

            def inc(sem, inst, n=1):
                v[sem] += n
                if inst is not None:
                    inst.then_inc(sems[sem], n)

            def w(name, sem, val):
                if on(name) and val > 0:
                    E[name].wait_ge(sems[sem], val)

            def dma(sem, engname, fn):
                w(engname, sem, v[sem])
                inst = fn() if on(engname) else None
                inc(sem, inst, 16)
                return v[sem]

            sy, ve, se, te, ge = (E.get(n) for n in ("sync", "vector", "scalar", "tensor", "gpsimd"))
            vv = _SS(ve, sems["mv"], v, "mv")
            ss = _SS(se, sems["msc"], v, "msc")

            dma("ld", "sync", lambda: sy.dma_start(out=sb["prow"][:], in_=ap["parts"]))
            dma("ld", "sync", lambda: sy.dma_start(out=sb["keys"][:], in_=ap["keysT"]))
            dma("ld", "sync", lambda: sy.dma_start(out=sb["iden_b"][:], in_=ap["idenb"]))
            ld_setup = dma("ld", "sync", lambda: sy.dma_start(
                out=sb["fln"][:].rearrange("p (c t) -> p c t", t=128),
                in_=ap["wqT"].rearrange("c p t -> p c t")))
            gd_dwn = dma("gd", "gpsimd", lambda: ge.dma_start(
                out=sb["dwn"][:], in_=ap["down"].rearrange("c p t -> p c t")))
            vv.memset(sb["eps"][:], EPS)
            vv.memset(sb["ones_r"][:], 1.0)
            w("vector", "ld", 16)
            sz = NC * IC
            while sz > IC:
                half = sz // 2
                vv.tensor_add(sb["prow"][:, 0:half], sb["prow"][:, 0:half],
                              sb["prow"][:, half:2 * half])
                sz = half
            vv.tensor_copy(sb["meanc"][:], sb["prow"][:, 0:IC])
            w("tensor", "ld", ld_setup)
            w("tensor", "mv", v["mv"])
            last = None
            for ic in range(IC):
                i = te.matmul(ps["qp"][:], sb["fln"][:, ic * 128:(ic + 1) * 128],
                              sb["meanc"][:, ic:ic + 1],
                              start=(ic == 0), stop=(ic == IC - 1)) if on("tensor") else None
                last = i
            inc("mp", last)
            w("scalar", "mp", v["mp"])
            scl = 1.0 / (float(T) * float(np.sqrt(128.0)))
            ss.activation(sb["q"][:], ps["qp"][:], AF.Copy, scale=scl)
            w("tensor", "msc", v["msc"])
            inc("mp", te.matmul(ps["sp"][:], sb["q"][:], sb["keys"][:], start=True, stop=True)
                if on("tensor") else None)
            w("scalar", "mp", v["mp"])
            ss.copy(sb["scr"][:], ps["sp"][:])
            w("vector", "msc", v["msc"])
            vv.reduce_max(sb["mx"][:], sb["scr"][:], AX.X, negate=True)
            w("scalar", "mv", v["mv"])
            ss.activation(sb["scr"][:], sb["scr"][:], AF.Exp, bias=sb["mx"][:])
            w("vector", "msc", v["msc"])
            vv.reduce_sum(sb["sm"][:], sb["scr"][:], AX.X)
            vv.reciprocal(sb["rs"][:], sb["sm"][:])
            vv.tensor_scalar(sb["scr"][:], sb["scr"][:], sb["rs"][:], None, ALU.mult)
            w("tensor", "mv", v["mv"])
            inc("mp", te.matmul(ps["bc"][:], sb["ones_r"][:], sb["scr"][:], start=True, stop=True)
                if on("tensor") else None)
            w("scalar", "mp", v["mp"])
            ss.copy(sb["cf"][:], ps["bc"][:])
            msc_cf = v["msc"]
            for ch in range(4):
                if ch >= 1:
                    w("gpsimd", "mv", v["mv"])
                gd_up = dma("gd", "gpsimd", lambda ch=ch: ge.dma_start(
                    out=sb["upstb"][:], in_=ap["up"][ch]))
                w("vector", "gd", gd_up)
                if ch == 0:
                    w("vector", "msc", msc_cf)
                for hf in range(2):
                    ee = ch * 2 + hf
                    vv.tensor_scalar(sb["upsc"][hf * 64:(hf + 1) * 64, ch, :],
                                     sb["upstb"][hf * 64:(hf + 1) * 64, :],
                                     sb["cf"][hf * 64:(hf + 1) * 64, ee:ee + 1], None,
                                     ALU.mult)

            upsc_ready = v["mv"]
            w("tensor", "gd", gd_dwn)
            mp_at_tile = []
            for tt in range(NT):
                xtp = "xt_l%d" % (tt % 2)
                if tt >= 2:
                    w("sync", "mp", mp_at_tile[tt - 2])
                xt_done = dma(xtp, "sync", lambda tt=tt: sy.dma_start(
                    out=sb["xtt%d" % (tt % 2)][:],
                    in_=ap["xn2T"][tt].rearrange("c p t -> p c t")))
                w("tensor", xtp, xt_done)
                last = None
                for ic in range(IC):
                    i = te.matmul(ps["hp"][:], sb["xtt%d" % (tt % 2)][:, ic, :], sb["dwn"][:, ic, :],
                                  start=(ic == 0), stop=(ic == IC - 1)) if on("tensor") else None
                    last = i
                inc("mp", last)
                mp_at_tile.append(v["mp"])
                w("scalar", "mp", v["mp"])
                ss.copy(sb["h"][:, tt, :], ps["hp"][:])
                w("tensor", "msc", v["msc"])
            for tt in range(NT):
                for ec in range(4):
                    inc("mp", te.transpose(ps["tp"][:], sb["h"][:, tt, ec * 128:(ec + 1) * 128],
                                           sb["iden_b"][:]) if on("tensor") else None)
                    w("scalar", "mp", v["mp"])
                    ss.copy(sb["hT"][:, ec, tt, :], ps["tp"][:])
                    w("tensor", "msc", v["msc"])

            mp_og_prev = 0
            mp_fb = {}
            msc_fb = {}
            fs_done = {}
            gt = 0
            for og in range(DG):
                w("gpsimd", "mp", mp_og_prev)
                w_done = dma("w_l", "gpsimd", lambda og=og: ge.dma_start(
                    out=sb["wbf"][:], in_=ap["wt"][og].rearrange("c p t -> p c t")))
                for tt in range(NT):
                    xtp = "xt_l%d" % (tt % 2)
                    if gt >= 2:
                        w("sync", "mp", mp_fb[gt - 2])
                    elif og == 0:
                        w("sync", "mp", mp_at_tile[NT - 2 + tt])
                    xt_done = dma(xtp, "sync", lambda tt=tt: sy.dma_start(
                        out=sb["xtt%d" % (tt % 2)][:],
                        in_=ap["xn2T"][tt].rearrange("c p t -> p c t")))
                    bb = tt % 2
                    w("tensor", xtp, xt_done)
                    w("tensor", "w_l", w_done)
                    if gt == 0:
                        w("tensor", "mv", upsc_ready)
                    if gt >= 2:
                        w("tensor", "msc", msc_fb[gt - 2])
                    last = None
                    for ic in range(IC):
                        i = te.matmul(ps["fb%d" % bb][:], sb["xtt%d" % (tt % 2)][:, ic, :],
                                      sb["wbf"][:, ic, :], start=(ic == 0), stop=False) \
                            if on("tensor") else None
                    for ec in range(4):
                        i = te.matmul(ps["fb%d" % bb][:], sb["hT"][:, ec, tt, :],
                                      sb["upsc"][:, ec, og * 512:(og + 1) * 512],
                                      start=False, stop=(ec == 3)) if on("tensor") else None
                        last = i
                    inc("mp", last)
                    mp_fb[gt] = v["mp"]
                    w("scalar", "mp", v["mp"])
                    if gt >= 2:
                        w("scalar", "fs", fs_done[gt - 2])
                    ss.copy(sb["fst%d" % bb][:], ps["fb%d" % bb][:])
                    if gt >= 1:
                        w("scalar", "mv", mv_sqs_prev)
                    ss.activation(sb["sqs"][:], sb["fst%d" % bb][:], AF.Square)
                    msc_fb[gt] = v["msc"]
                    w("vector", "msc", v["msc"])
                    vv.reduce_sum(sb["red"][:, tt, og, 0:1], sb["fst%d" % bb][:], AX.X)
                    vv.reduce_sum(sb["red"][:, tt, og, 1:2], sb["sqs"][:], AX.X)
                    mv_sqs_prev = v["mv"]
                    w("gpsimd", "mv", v["mv"])
                    fs_done[gt] = dma("fs", "gpsimd", lambda tt=tt, og=og, bb=bb: ge.dma_start(
                        out=ap["fscr"][tt * 128:(tt + 1) * 128, og * 512:(og + 1) * 512],
                        in_=sb["fst%d" % bb][:]))
                    gt += 1
                mp_og_prev = v["mp"]

            mv_p2 = 0
            for tt in range(NT):
                if tt == 0:
                    w("sync", "fs", v["fs"])
                if tt >= 1:
                    w("sync", "mv", mv_p2)
                    w("sync", "od", v["od"])
                dma("ld", "sync", lambda tt=tt: sy.dma_start(
                    out=sb["fln"][:], in_=ap["fscr"][tt * 128:(tt + 1) * 128, :]))
                ld_p2 = dma("ld", "sync", lambda tt=tt: sy.dma_start(
                    out=sb["x1c"][:], in_=ap["x1"][tt * 128:(tt + 1) * 128, :]))
                w("vector", "ld", ld_p2)
                vv.reduce_sum(sb["st_s"][:], sb["red"][:, tt, :, 0], AX.X)
                vv.reduce_sum(sb["st_q"][:], sb["red"][:, tt, :, 1], AX.X)
                vv.tensor_scalar(sb["mn"][:], sb["st_s"][:], 1.0 / D, None, ALU.mult)
                vv.tensor_scalar(sb["vr"][:], sb["st_q"][:], 1.0 / D, None, ALU.mult)
                vv.tensor_mul(sb["st_s"][:], sb["mn"][:], sb["mn"][:])
                vv.tensor_sub(sb["vr"][:], sb["vr"][:], sb["st_s"][:])
                w("scalar", "mv", v["mv"])
                ss.activation(sb["sd"][:], sb["vr"][:], AF.Sqrt, bias=sb["eps"][:])
                w("vector", "msc", v["msc"])
                vv.reciprocal(sb["rr"][:], sb["sd"][:])
                vv.tensor_scalar(sb["fln"][:], sb["fln"][:], sb["mn"][:], sb["rr"][:],
                                 ALU.subtract, ALU.mult)
                vv.tensor_add(sb["x1c"][:], sb["fln"][:], sb["x1c"][:])
                mv_p2 = v["mv"]
                w("gpsimd", "mv", v["mv"])
                dma("od", "gpsimd", lambda tt=tt: ge.dma_start(
                    out=ap["out"][tt * 128:(tt + 1) * 128, :], in_=sb["x1c"][:]))
            w("gpsimd", "od", v["od"])

        @blk.sync
        def _(sync):
            walker({"sync": sync})

        @blk.vector
        def _(vector):
            walker({"vector": vector})

        @blk.scalar
        def _(scalar):
            walker({"scalar": scalar})

        @blk.tensor
        def _(tensor):
            walker({"tensor": tensor})

        @blk.gpsimd
        def _(gpsimd):
            walker({"gpsimd": gpsimd})

    return nc


def kernel(**inputs):
    x = np.ascontiguousarray(np.asarray(inputs["L0_attn_x"], dtype=np.float32).reshape(T, D))
    bvq = np.ascontiguousarray(np.asarray(inputs["L0_attn_bv_q"], dtype=np.float32))
    bvk = np.ascontiguousarray(np.asarray(inputs["L0_attn_bv_k"], dtype=np.float32))
    bvv = np.ascontiguousarray(np.asarray(inputs["L0_attn_bv_v"], dtype=np.float32))
    W = np.asarray(inputs["L0_ffn_W_org"], dtype=np.float32)
    ldn = np.asarray(inputs["L0_ffn_lora_down"], dtype=np.float32)
    lup = np.asarray(inputs["L0_ffn_lora_up"], dtype=np.float32)
    lk = np.asarray(inputs["L0_ffn_lora_keys"], dtype=np.float32)
    wq = np.asarray(inputs["L0_ffn_Wq"], dtype=np.float32)

    ncA = build_A()
    inA = [{"x": np.ascontiguousarray(x[c * TOK_C:(c + 1) * TOK_C])} for c in range(NC)]
    resA = run_bass_kernel_spmd(ncA, inA, core_ids=list(range(NC)))
    xn1 = np.concatenate([resA.results[c]["xn1"] for c in range(NC)], axis=0)

    xn1T = np.ascontiguousarray(xn1.T)
    xn1T_t = np.ascontiguousarray(
        xn1T.reshape(IC, 128, T // QT, QT).transpose(2, 0, 1, 3))
    xn1w = np.ascontiguousarray(
        xn1.reshape(NWV, WAVE * QT, DG, 512).transpose(0, 2, 1, 3))
    iden = np.eye(128, dtype=np.float32)
    inB = []
    tril = np.triu(np.ones((QT, QT), dtype=np.float32)).astype(BF16NP)
    for c in range(NC):
        gl = _g_list(c)
        xq = np.ascontiguousarray(
            np.stack([x[g * QT:(g + 1) * QT] for g in gl], axis=0))
        xqT = np.ascontiguousarray(
            np.stack([x[g * QT:(g + 1) * QT].T.reshape(IC, 128, QT) for g in gl], axis=0))
        masks = np.zeros((TOTU, QT, QT), dtype=BF16NP)
        seq = 0
        for k in range(K_PH):
            g = gl[k]
            masks[seq:seq + g] = BF16NP(1.0)
            masks[seq + g] = tril
            seq += _p_k(k)
        inB.append({
            "xn1T": xn1T_t, "xn1w": xn1w, "xq": xq, "xqT": xqT, "masks": masks,
            "bvq": bvq, "bvk": bvk, "bvv": bvv, "iden": iden,
        })
    ncB = build_B()
    resB = run_bass_kernel_spmd(ncB, inB, core_ids=list(range(NC)))

    parts = np.ascontiguousarray(np.concatenate(
        [resB.results[c]["parto"] for c in range(NC)], axis=1))
    WTt = np.ascontiguousarray(W.reshape(DG, 512, IC, 128).transpose(0, 2, 3, 1))
    down_t = np.ascontiguousarray(ldn.transpose(1, 0, 2).reshape(IC, 128, NC * 64))
    up_r = np.ascontiguousarray(lup.reshape(NC * 64, D).reshape(4, 128, D))
    wqT = np.ascontiguousarray(wq.T.reshape(IC, 128, 128))
    keysT = np.ascontiguousarray(lk.T)
    idenb = np.eye(128, dtype=BF16NP)
    inC = []
    for c in range(NC):
        x1 = np.ascontiguousarray(resB.results[c]["x1o"].reshape(TOK_C, D))
        xn2 = resB.results[c]["xn2o"].reshape(TOK_C, D)
        xn2T_t = np.ascontiguousarray(
            xn2.reshape(TOK_C // 128, 128, IC, 128).transpose(0, 2, 3, 1))
        inC.append({
            "xn2T": xn2T_t, "x1": x1, "parts": parts, "wt": WTt, "down": down_t,
            "up": up_r, "wqT": wqT, "keysT": keysT, "idenb": idenb,
        })
    ncC = build_C()
    resC = run_bass_kernel_spmd(ncC, inC, core_ids=list(range(NC)))

    out = np.empty((T, D), dtype=np.float32)
    for c in range(NC):
        gl = _g_list(c)
        oc = resC.results[c]["out"].reshape(K_PH, QT, D)
        for k in range(K_PH):
            out[gl[k] * QT:(gl[k] + 1) * QT] = oc[k]
    return out.reshape(1, T, D)
